# version 01
# baseline (speedup 1.0000x reference)
import dataclasses
import numpy as np
import bass_rust
import concourse.bass as bass
import concourse.mybir as mybir
from concourse.bass_utils import run_bass_kernel_spmd
from concourse.expressions_rust import make_scalar_value

F32 = mybir.dt.float32
BF16 = mybir.dt.bfloat16
AF = mybir.ActivationFunctionType
ALU = mybir.AluOpType
AX = mybir.AxisListType

D, NCH, T, S, H, NCORES = 2048, 16, 1024, 8192, 16, 8
EPS_LN, EPS_RMS, EPS_LA = 1e-5, 1e-6, 1e-6
ISQ = float(1.0 / np.sqrt(128.0))
MARGIN = 40.0


def ro(ap, off):
    if isinstance(off, int):
        return dataclasses.replace(ap, offset=ap.offset + off)
    return dataclasses.replace(ap, offset=off + ap.offset)



def strips_xw(w):
    din, dout = w.shape
    nk, nm = din // 128, dout // 128
    return np.ascontiguousarray(
        w.reshape(nk, 128, nm, 128).transpose(2, 1, 0, 3).reshape(nm, 128, nk * 128))


def strips_xwT(w):
    return strips_xw(np.ascontiguousarray(w.T))


def host_inputs(inp):
    x = np.asarray(inp["L0_attn_x"], np.float32)[0]
    sh = {}
    sh["wq_l"] = strips_xw(np.asarray(inp["L0_attn_Wq"], np.float32))
    sh["wk_l"] = strips_xw(np.asarray(inp["L0_attn_Wk"], np.float32))
    sh["wv_l"] = strips_xw(np.asarray(inp["L0_attn_Wv"], np.float32))
    sh["wo_l"] = strips_xw(np.asarray(inp["L0_attn_Wo"], np.float32))
    sh["wq_f"] = strips_xwT(np.asarray(inp["L0_ffn_Wq"], np.float32))
    sh["wk_f"] = strips_xwT(np.asarray(inp["L0_ffn_Wk"], np.float32))
    sh["wv_f"] = strips_xwT(np.asarray(inp["L0_ffn_Wv"], np.float32))
    sh["wo_f"] = strips_xw(np.asarray(inp["L0_ffn_Wo"], np.float32))
    w_in = strips_xwT(np.asarray(inp["L0_ffn_W_in_base"], np.float32))
    Bm = np.asarray(inp["L0_ffn_B"], np.float32)
    bt = np.zeros((64, 128, 1, 128), np.float32)
    bt[:, :16, 0, :] = Bm.reshape(16, 64, 128).transpose(1, 0, 2)
    sh["w_g"] = np.ascontiguousarray(
        np.concatenate([w_in.reshape(64, 128, 16, 128), bt], axis=2).reshape(64, 128, 17 * 128))
    sh["w_u"] = strips_xwT(np.asarray(inp["L0_ffn_W_up_base"], np.float32))
    sh["w_dn"] = strips_xwT(np.asarray(inp["L0_ffn_W_down"], np.float32))
    for nm_, key in [("bq_l", "L0_attn_bq"), ("bk_l", "L0_attn_bk"),
                     ("bv_l", "L0_attn_bv"), ("bo_l", "L0_attn_bo"),
                     ("w1_f", "L0_ffn_ln1_w"), ("w2_f", "L0_ffn_ln2_w")]:
        sh[nm_] = np.ascontiguousarray(np.asarray(inp[key], np.float32).reshape(16, 128).T)
    sh["a_lr"] = np.ascontiguousarray(
        np.asarray(inp["L0_ffn_A"], np.float32).reshape(16, 128, 16).transpose(1, 0, 2).reshape(128, 256))
    kk = np.arange(128)
    sh["ident"] = np.eye(128, dtype=np.float32)
    sh["tri_m"] = (kk[:, None] <= kk[None, :]).astype(np.float32)
    maps = []
    for c in range(NCORES):
        a0, b0 = c * 512, (15 - c) * 512
        xs = np.concatenate([x[a0:a0 + 512], x[b0:b0 + 512]], 0)
        m = dict(sh)
        m["x_fm"] = np.ascontiguousarray(xs.T)
        maps.append(m)
    return maps


def host_output(results, key="out_fm", dfeat=D):
    out = np.empty((1, S, dfeat), np.float32)
    for c in range(NCORES):
        shd = np.asarray(results[c][key], np.float32)
        a0, b0 = c * 512, (15 - c) * 512
        out[0, a0:a0 + 512] = shd[:, 0:512].T
        out[0, b0:b0 + 512] = shd[:, 512:1024].T
    return out



class B:

    def __init__(self, debug=None):
        self.debug = debug
        nc = self.nc = bass.Bass("TRN2", target_bir_lowering=False)
        self.ops = {e: [] for e in ("sync", "gpsimd", "tensor", "vector", "scalar")}
        self.cnt = {}

        def DI(name, shape, dt=F32):
            return nc.dram_tensor(name, shape, dt, kind="ExternalInput").ap()
        self.x_fm = DI("x_fm", [D, T])
        self.w = {k: DI(k, [16, 128, 2048]) for k in
                  ("wq_l", "wk_l", "wv_l", "wo_l", "wq_f", "wk_f", "wv_f", "wo_f")}
        self.w["w_g"] = DI("w_g", [64, 128, 17 * 128])
        self.w["w_u"] = DI("w_u", [64, 128, 2048])
        self.w["w_dn"] = DI("w_dn", [16, 128, 64 * 128])
        self.bias_i = {k: DI(k, [128, 16]) for k in
                       ("bq_l", "bk_l", "bv_l", "bo_l", "w1_f", "w2_f")}
        self.a_lr = DI("a_lr", [128, 256])
        self.ident_i = DI("ident", [128, 128])
        self.tri_i = DI("tri_m", [128, 128])
        self.out_fm = nc.dram_tensor("out_fm", [D, T], F32, kind="ExternalOutput").ap()

        self.lin_cin = nc.dram_tensor("lin_cin", [2, 128, 2064], F32)
        self.lin_cout = nc.dram_tensor("lin_cout", [17, 2, 128, 2064], F32)
        self.k_cin = nc.dram_tensor("k_cin", [2, 16, 128, 512], BF16)
        self.k_cout = nc.dram_tensor("k_cout", [8, 2, 16, 128, 512], BF16)
        self.v_cin = nc.dram_tensor("v_cin", [2, 16, 4, 128, 129], BF16)
        self.v_cout = nc.dram_tensor("v_cout", [8, 2, 16, 4, 128, 129], BF16)
        self.n_cin = nc.dram_tensor("n_cin", [1, 16], F32)
        self.n_cout = nc.dram_tensor("n_cout", [8, 16], F32)
        self.y_spill = nc.dram_tensor("y_spill", [D, T], F32)

        self.dbg = None
        if debug is not None:
            shp = {"ln0": [D, T], "qkv": [3 * D, T], "attn_lin": [D, T], "y1": [D, T],
                   "x1": [D, T], "qkv_f": [3 * D, T], "attn_f": [D, T],
                   "mod": [16, T], "h2": [D, T]}[debug]
            self.dbg = nc.dram_tensor("dbg", shp, F32, kind="ExternalOutput").ap()

        self._off = 16512

        def place(name, shape, dtype, at=None):
            nbytes = int(np.prod(shape[1:])) * (4 if dtype == F32 else 2)
            if at is None:
                at = self._off
            at_al = ((at + 63) // 64) * 64
            t = nc.alloc_sbuf_tensor_at(name, shape, dtype, offset=at_al)
            self._off = max(self._off, at_al + nbytes)
            return t.ap()
        self.place = place

        wr0 = self._off
        self.wring = place("wring", [128, 2 * 2176], BF16)
        self.probs = place("probs", [128, 2 * 512], BF16, at=wr0)
        self.kbuf = place("kbuf", [128, 2 * 512], BF16, at=wr0 + 2048)
        self.vbuf = place("vbuf", [128, 2 * 516], BF16, at=wr0 + 4096)
        self.consts = place("consts", [128, 104], F32)
        self.a_sb = place("a_sb", [128, 256], BF16)
        self.ident = place("ident_sb", [128, 128], BF16)
        self.tri = place("tri_sb", [128, 128], BF16)
        self.onescol = place("onescol", [128, 8], BF16)
        self.onesrow = place("onesrow", [1, 128], BF16)
        self.zeros128 = place("zeros128", [128, 128], BF16)
        self.woa = place("woa", [128, 256], BF16)
        self.negm = place("negm", [128, 16], F32)
        self.statq = place("statq", [1, 16], F32)
        self.statk = place("statk", [1, 16], F32)
        self.statm = place("statm", [1, 16], F32)
        self.statbf = place("statbf", [1, 32], BF16)
        self.nstage = place("nstage", [1, 128], F32)
        self.evac = place("evac", [128, 2048], F32)
        ebf0 = self._off
        self.ebf = place("ebf", [128, 2080], BF16)
        self.state_bf = place("state_bf", [128, 2064], BF16, at=ebf0)
        self.headring = place("headring", [128, 2048], BF16, at=ebf0)
        self.rings = place("rings", [128, 2048], BF16)
        uni = self._off
        self.slot_tot = place("slot_tot", [128, 2 * 2064], F32, at=uni)
        self.kvg = place("kvg", [128, 2064], F32, at=uni)
        self.state = place("state", [128, 2064], F32, at=uni + 8256)
        self.modsb = place("modsb", [16, 1024], F32, at=uni)
        self.modha = place("modha", [128, 1024], BF16, at=uni + 4096)
        self.bc_mu8 = place("bc_mu8", [128, 1024], F32, at=uni + 6400)
        self.bc_s38 = place("bc_s38", [128, 1024], F32, at=uni + 10496)
        self._off = uni + 16512
        big = ((self._off + 63) // 64) * 64
        self.kf = place("kf", [128, H * T], BF16, at=big)
        self.vfm = place("vfm", [128, H * T], BF16, at=big + 32768)
        self.qf = place("qf", [128, H * T], BF16, at=big + 65536)
        self.attn = place("attn", [128, H * T], BF16, at=big + 98304)
        self.bc_mu = place("bc_mu", [128, 1024], F32, at=big + 98304)
        self.bc_c1 = place("bc_c1", [128, 1024], F32, at=big + 98304 + 4096)
        self.gu = place("gu", [128, 64 * T], BF16, at=big)
        r1 = big + 131072
        self.ln0 = place("ln0", [128, NCH * T], BF16, at=r1)
        self.vtm = place("vtm", [128, 8 * 16 * 128], BF16, at=r1)
        self.wo_sb = place("wo_sb", [128, NCH * T], BF16, at=r1)
        self.x1 = place("x1", [128, NCH * T], BF16, at=r1)
        self.hbuf = place("hbuf", [128, NCH * T], BF16, at=r1)
        self.frbuf = place("frbuf", [128, NCH * T], BF16, at=r1)
        top = r1 + 32768
        assert top <= 229344, f"sbuf over budget: {top}"

        self.psA = [nc.alloc_psum_tensor(f"psA{i}", [128, 512], F32).ap() for i in range(2)]
        self.psB = nc.alloc_psum_tensor("psB", [128, 2048], F32).ap()
        self.psT = [nc.alloc_psum_tensor(f"psT{i}", [128, 1024], BF16).ap() for i in range(2)]

        self.sem = {k: nc.alloc_semaphore(k) for k in
                    ("d_sy", "d_s2", "d_gp", "d_g2", "te", "ve", "se", "cc", "ge")}
        self._cid = {}
        self._cidreg = {}
        ET = mybir.EngineType
        self.rgp = [nc.alloc_registers(f"rgp{i}", engines=(ET.Pool,))[ET.Pool]
                    for i in range(4)]
        self.rpe = nc.alloc_registers("rpe", engines=(ET.PE,))[ET.PE]

    def get(self, s):
        return self.cnt.get(s, 0)

    def inc(self, s, n=1):
        self.cnt[s] = self.cnt.get(s, 0) + n
        return self.cnt[s]

    def on(self, eng, fn):
        self.ops[eng].append(fn)

    def cid(self, engname, eng):
        if engname not in self._cid:
            self._cid[engname] = eng.partition_id()
        return self._cid[engname]

    def cidreg(self, engname, eng):
        if engname not in self._cidreg:
            self._cidreg[engname] = eng.to_reg(self.cid(engname, eng))
        return self._cidreg[engname]

    def barrier(self):
        tgt = [(k, self.cnt.get(k, 0)) for k in self.sem if self.cnt.get(k, 0) > 0]
        sems = self.sem
        for en in self.ops:
            def mk(tgt=list(tgt)):
                def f(eng):
                    for k, v in tgt:
                        eng.wait_ge(sems[k], v)
                return f
            self.on(en, mk())

    def finish(self):
        nc, sems, cnt = self.nc, self.sem, self.cnt
        with nc.Block() as block:
            @block.sync
            def _(sync):
                for f in self.ops["sync"]:
                    f(sync)

            @block.gpsimd
            def _(gp):
                for f in self.ops["gpsimd"]:
                    f(gp)
                for k, s in sems.items():
                    if cnt.get(k, 0) > 0:
                        gp.wait_ge(s, cnt[k])

            @block.tensor
            def _(t):
                for f in self.ops["tensor"]:
                    f(t)

            @block.vector
            def _(v):
                for f in self.ops["vector"]:
                    f(v)

            @block.scalar
            def _(s):
                for f in self.ops["scalar"]:
                    f(s)
        return nc

    def dump_fm(self, src_bf_fn, n_ch, dst):
        ve0 = self.get("ve")
        sy0 = [self.get("d_sy"), self.get("d_s2")]
        dS = ["d_sy", "d_s2"]

        def f_v(v):
            for kc in range(n_ch):
                if kc >= 2:
                    v.wait_ge(self.sem[dS[kc % 2]], sy0[kc % 2] + 16 * (kc // 2))
                v.tensor_copy(self.evac[:, (kc % 2) * 1024:(kc % 2) * 1024 + 1024],
                              src_bf_fn(kc)).then_inc(self.sem["ve"], 1)
        self.on("vector", f_v)
        self.inc("ve", n_ch)

        def f_s(sync):
            for kc in range(n_ch):
                sync.wait_ge(self.sem["ve"], ve0 + kc + 1)
                sync.dma_start(out=dst[kc * 128:(kc + 1) * 128, :],
                               in_=self.evac[:, (kc % 2) * 1024:(kc % 2) * 1024 + 1024]
                               ).then_inc(self.sem[dS[kc % 2]], 16)
        self.on("sync", f_s)
        self.inc("d_sy", 16 * ((n_ch + 1) // 2))
        self.inc("d_s2", 16 * (n_ch // 2))
        self.barrier()

    def ln_pass(self, src_hbm):
        sy0 = [self.get("d_sy"), self.get("d_s2")]
        ve0, se0, te0 = (self.get(k) for k in ("ve", "se", "te"))
        sm = self.sem
        dS = ["d_sy", "d_s2"]

        def f_sync(sync):
            for kc in range(NCH):
                if kc >= 2:
                    sync.wait_ge(sm["ve"], ve0 + kc - 1)
                sync.dma_start(out=self.evac[:, (kc % 2) * 1024:(kc % 2) * 1024 + 1024],
                               in_=src_hbm[kc * 128:(kc + 1) * 128, :]).then_inc(sm[dS[kc % 2]], 16)
        self.on("sync", f_sync)
        self.inc("d_sy", 16 * (NCH // 2))
        self.inc("d_s2", 16 * (NCH // 2))

        def f_ve(v):
            for kc in range(NCH):
                v.wait_ge(sm[dS[kc % 2]], sy0[kc % 2] + (kc // 2 + 1) * 16)
                if kc >= 2:
                    v.wait_ge(sm["te"], te0 + kc - 1)
                v.tensor_copy(self.ebf[:, (kc % 2) * 1024:(kc % 2) * 1024 + 1024],
                              self.evac[:, (kc % 2) * 1024:(kc % 2) * 1024 + 1024]
                              ).then_inc(sm["ve"], 1)
        self.on("vector", f_ve)
        self.inc("ve", NCH)

        def f_sc(s):
            for kc in range(NCH):
                s.wait_ge(sm["ve"], ve0 + kc + 1)
                if kc >= 2:
                    s.wait_ge(sm["te"], te0 + kc - 1)
                s.square(self.rings[:, (kc % 2) * 1024:(kc % 2) * 1024 + 1024],
                         self.ebf[:, (kc % 2) * 1024:(kc % 2) * 1024 + 1024]).then_inc(sm["se"], 1)
        self.on("scalar", f_sc)
        self.inc("se", NCH)

        def f_te(t):
            for kc in range(NCH):
                t.wait_ge(sm["se"], se0 + kc + 1)
                last = None
                for half in range(2):
                    sl = (kc % 2) * 1024 + half * 512
                    t.matmul(self.psB[0:1, half * 512:half * 512 + 512], self.onescol[:, 0:1],
                             self.ebf[:, sl:sl + 512], start=(kc == 0), stop=(kc == NCH - 1))
                    last = t.matmul(self.psB[0:1, 1024 + half * 512:1024 + half * 512 + 512],
                             self.onescol[:, 0:1],
                             self.rings[:, sl:sl + 512], start=(kc == 0), stop=(kc == NCH - 1))
                last.then_inc(sm["te"], 1)
        self.on("tensor", f_te)
        self.inc("te", NCH)

    def row_seq(self, steps):
        sm = self.sem
        chain = []
        prev = None
        for engname, fn in steps:
            s = "ve" if engname == "vector" else ("se" if engname == "scalar" else "te")
            waits = [prev] if prev is not None else []
            tgt = self.inc(s, 1)
            chain.append((engname, fn, list(waits), (s, tgt)))
            prev = (s, tgt)
        for engname, fn, waits, (s, tgt) in chain:
            def mk(fn=fn, waits=list(waits), s=s):
                def f(eng):
                    for ws, wv in waits:
                        eng.wait_ge(sm[ws], wv)
                    fn(eng).then_inc(sm[s], 1)
                return f
            self.on(engname, mk())

    def ln_rows(self, extra_rms, extra_dsts=False):
        sm = self.sem
        te_done = self.get("te")
        mu = self.evac[0:1, 0:1024]
        inv = self.evac[0:1, 1024:2048]
        var = self.bc_c1[0:1, 0:1024]
        sd = self.bc_mu[0:1, 0:1024]

        def w_te(v):
            v.wait_ge(sm["te"], te_done)
            return v.tensor_scalar_mul(mu, self.psB[0:1, 0:1024], 1.0 / D)
        steps = [
            ("vector", w_te),
            ("vector", lambda v: v.tensor_scalar_mul(var, self.psB[0:1, 1024:2048], 1.0 / D)),
            ("vector", lambda v: v.tensor_mul(sd, mu, mu)),
            ("vector", lambda v: v.tensor_sub(var, var, sd)),
            ("scalar", lambda s: s.activation(sd, var, AF.Sqrt, bias=self.consts[0:1, 96:97], scale=1.0)),
            ("vector", lambda v: v.reciprocal(inv, sd)),
        ]
        s3row = self.psB[0:1, 1024:2048]
        s3save = self.evac[0:1, 0:0]
        if extra_dsts:
            steps += [("vector", lambda v: v.tensor_copy(self.psB[0:1, 0:1024], inv))]
        if extra_rms:
            steps += [
                ("vector", lambda v: v.tensor_mul(var, var, inv)),
                ("vector", lambda v: v.tensor_mul(var, var, inv)),
                ("scalar", lambda s: s.activation(sd, var, AF.Sqrt, bias=self.consts[0:1, 97:98], scale=1.0)),
                ("vector", lambda v: v.reciprocal(var, sd)),
                ("vector", lambda v: v.tensor_mul(inv, inv, var)),
            ]
        pairs = [(mu, "bc_mu"), (inv, "bc_c1")]
        if extra_dsts:
            pairs += [(mu, "bc_mu8"), (self.psB[0:1, 0:1024], "bc_s38")]
        for (src, dstname) in pairs:
            dst = getattr(self, dstname)
            hi = self.ebf[0:1, 0:1024]
            lo = self.ebf[0:1, 1024:2048]
            steps += [
                ("vector", lambda v, src=src, hi=hi: v.tensor_copy(hi, src)),
                ("vector", lambda v, src=src, hi=hi, lo=lo: v.tensor_sub(lo, src, hi)),
            ]
            for half in range(2):
                def mmf(t, half=half, hi=hi, lo=lo):
                    t.matmul(self.psA[0][:, 0:512], self.onesrow[0:1, :],
                             hi[0:1, half * 512:half * 512 + 512], start=True, stop=False)
                    return t.matmul(self.psA[0][:, 0:512], self.onesrow[0:1, :],
                                    lo[0:1, half * 512:half * 512 + 512], start=False, stop=True)
                steps.append(("tensor", mmf))
                steps.append(("vector", lambda v, half=half, dst=dst:
                              v.tensor_copy(dst[:, half * 512:half * 512 + 512],
                                            self.psA[0][:, 0:512])))
        self.row_seq(steps)
        self.barrier()

    def ln_norm(self, src_hbm, dst_bf_fn, gaincol):
        sm = self.sem
        sy0 = [self.get("d_sy"), self.get("d_s2")]
        ve0, se0 = self.get("ve"), self.get("se")
        dS = ["d_sy", "d_s2"]

        def f_sync(sync):
            for kc in range(NCH):
                if kc >= 2:
                    if gaincol is None:
                        sync.wait_ge(sm["ve"], ve0 + 2 * (kc - 1))
                    else:
                        sync.wait_ge(sm["se"], se0 + kc - 1)
                sync.dma_start(out=self.evac[:, (kc % 2) * 1024:(kc % 2) * 1024 + 1024],
                               in_=src_hbm[kc * 128:(kc + 1) * 128, :]).then_inc(sm[dS[kc % 2]], 16)
        self.on("sync", f_sync)
        self.inc("d_sy", 16 * (NCH // 2))
        self.inc("d_s2", 16 * (NCH // 2))

        def f_ve(v):
            for kc in range(NCH):
                v.wait_ge(sm[dS[kc % 2]], sy0[kc % 2] + (kc // 2 + 1) * 16)
                sl = (kc % 2) * 1024
                v.tensor_sub(self.evac[:, sl:sl + 1024], self.evac[:, sl:sl + 1024],
                             self.bc_mu).then_inc(sm["ve"], 1)
                v.wait_ge(sm["ve"], ve0 + 2 * kc + 1)
                if gaincol is None:
                    v.tensor_mul(dst_bf_fn(kc), self.evac[:, sl:sl + 1024],
                                 self.bc_c1).then_inc(sm["ve"], 1)
                else:
                    v.tensor_mul(self.evac[:, sl:sl + 1024], self.evac[:, sl:sl + 1024],
                                 self.bc_c1).then_inc(sm["ve"], 1)
        self.on("vector", f_ve)
        self.inc("ve", 2 * NCH)

        if gaincol is not None:
            def f_sc(s):
                for kc in range(NCH):
                    s.wait_ge(sm["ve"], ve0 + 2 * kc + 2)
                    s.activation(dst_bf_fn(kc), self.evac[:, (kc % 2) * 1024:(kc % 2) * 1024 + 1024],
                                 AF.Copy, bias=0.0,
                                 scale=self.consts[:, gaincol * 16 + kc:gaincol * 16 + kc + 1]
                                 ).then_inc(sm["se"], 1)
            self.on("scalar", f_sc)
            self.inc("se", NCH)
        self.barrier()

    def w_proj(self, wsrc, nstrips, tiles_per_strip, rhs_fn, nacc, evac_fn,
               extra_mm=None, mm_n=512, nhalf=2, ve_per_evac=1):
        sm = self.sem
        ncols = tiles_per_strip * 128
        te0, ve0 = self.get("te"), self.get("ve")
        gp0 = [self.get("d_gp"), self.get("d_g2")]
        dG = ["d_gp", "d_g2"]

        def f_gp(gp):
            for i in range(nstrips):
                if i >= 2:
                    gp.wait_ge(sm["te"], te0 + i - 1)
                gp.dma_start(out=self.wring[:, (i % 2) * 2176:(i % 2) * 2176 + ncols],
                             in_=wsrc[i, :, 0:ncols]).then_inc(sm[dG[i % 2]], 16)
        self.on("gpsimd", f_gp)
        self.inc("d_gp", 16 * ((nstrips + 1) // 2))
        self.inc("d_g2", 16 * (nstrips // 2))

        def f_te(t):
            for m in range(nstrips):
                t.wait_ge(sm[dG[m % 2]], gp0[m % 2] + (m // 2 + 1) * 16)
                if m >= 2:
                    t.wait_ge(sm["ve"], ve0 + (m - 1) * nhalf * ve_per_evac)
                slot = (m % 2) * 2176
                last = None
                for half in range(nhalf):
                    out = self.psB[:, (m % 2) * 1024 + half * 512:(m % 2) * 1024 + half * 512 + mm_n]
                    for k in range(nacc):
                        last = t.matmul(out, self.wring[:, slot + k * 128:slot + (k + 1) * 128],
                                 rhs_fn(k, half), start=(k == 0),
                                 stop=(k == nacc - 1 and extra_mm is None))
                    if extra_mm is not None:
                        last = extra_mm(t, m, half, out, slot)
                last.then_inc(sm["te"], 1)
        self.on("tensor", f_te)
        self.inc("te", nstrips)

        for m in range(nstrips):
            for half in range(nhalf):
                evac_fn(m, half,
                        self.psB[:, (m % 2) * 1024 + half * 512:(m % 2) * 1024 + half * 512 + mm_n],
                        te0 + m + 1)
        self.barrier()

    def phase0(self):
        sm = self.sem

        def f_sync(sync):
            for i, key in enumerate(["bq_l", "bk_l", "bv_l", "bo_l", "w1_f", "w2_f"]):
                sync.dma_start(out=self.consts[:, i * 16:(i + 1) * 16],
                               in_=self.bias_i[key]).then_inc(sm["d_sy"], 16)
        self.on("sync", f_sync)
        self.inc("d_sy", 96)

        def f_gp(gp):
            gp.dma_start(out=self.ident[:], in_=self.ident_i).then_inc(sm["d_gp"], 16)
            gp.dma_start(out=self.tri[:], in_=self.tri_i).then_inc(sm["d_gp"], 16)
            gp.dma_start(out=self.a_sb[:], in_=self.a_lr).then_inc(sm["d_gp"], 16)
        self.on("gpsimd", f_gp)
        self.inc("d_gp", 48)

        def f_ve(v):
            v.memset(self.evac[:, 0:2048], 0.0).then_inc(sm["ve"], 1)
            v.memset(self.onescol[:], 1.0).then_inc(sm["ve"], 1)
            v.memset(self.onesrow[:], 1.0).then_inc(sm["ve"], 1)
            v.memset(self.zeros128[:], 0.0).then_inc(sm["ve"], 1)
            v.memset(self.consts[:, 96:97], EPS_LN).then_inc(sm["ve"], 1)
            v.memset(self.consts[:, 97:98], EPS_RMS).then_inc(sm["ve"], 1)
            v.memset(self.consts[:, 98:99], EPS_LA).then_inc(sm["ve"], 1)
            v.memset(self.slot_tot[:], 0.0).then_inc(sm["ve"], 1)
        self.on("vector", f_ve)
        self.inc("ve", 8)
        ve_z = self.get("ve")

        def f_sz(sync):
            sync.wait_ge(sm["ve"], ve_z)
            sync.dma_start(out=self.lin_cout.ap()[16, 0][:, 0:2048],
                           in_=self.evac[:, 0:2048]).then_inc(sm["d_sy"], 16)
            sync.dma_start(out=self.lin_cout.ap()[16, 0][:, 2048:2064],
                           in_=self.evac[:, 0:16]).then_inc(sm["d_sy"], 16)
        self.on("sync", f_sz)
        self.inc("d_sy", 32)
        self.barrier()

    def phase1(self):
        self.ln_pass(self.x_fm)
        self.ln_rows(extra_rms=False)
        self.ln_norm(self.x_fm, lambda kc: self.ln0[:, kc * T:(kc + 1) * T], None)
        if self.debug == "ln0":
            self.dump_fm(lambda kc: self.ln0[:, kc * T:(kc + 1) * T], NCH, self.dbg)
            return True
        return False

    def phaseA2(self):
        sm = self.sem

        def rhs(k, half):
            return self.ln0[:, k * T + half * 512:k * T + half * 512 + 512]

        def make_elu_evac(dstbuf, biascol):
            def evac_fn(m, half, ps, te_tgt):
                sm_ = sm
                ve_t = self.inc("ve", 1)
                se_t = self.inc("se", 1)
                ve_t2 = self.inc("ve", 1)
                ve_t3 = self.inc("ve", 1)
                sl = (half % 2) * 1024
                bias_ap = self.consts[:, biascol * 16 + m:biascol * 16 + m + 1]
                dst = dstbuf[:, m * T + half * 512:m * T + half * 512 + 512]

                def f_v(v):
                    v.wait_ge(sm_["te"], te_tgt)
                    v.tensor_scalar(self.evac[:, sl:sl + 512], ps, bias_ap, 0.0,
                                    ALU.add, ALU.min).then_inc(sm_["ve"], 1)
                    v.wait_ge(sm_["se"], se_t)
                    v.tensor_scalar(self.evac[:, sl + 512:sl + 1024], ps, bias_ap, 0.0,
                                    ALU.add, ALU.max).then_inc(sm_["ve"], 1)
                    v.wait_ge(sm_["ve"], ve_t2)
                    v.tensor_add(dst, self.evac[:, sl + 512:sl + 1024],
                                 self.ebf[:, sl:sl + 512]).then_inc(sm_["ve"], 1)
                self.on("vector", f_v)

                def f_s(s):
                    s.wait_ge(sm_["ve"], ve_t)
                    s.activation(self.ebf[:, sl:sl + 512], self.evac[:, sl:sl + 512],
                                 AF.Exp, bias=0.0, scale=1.0).then_inc(sm_["se"], 1)
                self.on("scalar", f_s)
            return evac_fn

        self.w_proj(self.w["wq_l"], 16, 16, rhs, 16, make_elu_evac(self.qf, 0), ve_per_evac=3)
        self.w_proj(self.w["wk_l"], 16, 16, rhs, 16, make_elu_evac(self.kf, 1), ve_per_evac=3)

        def v_evac(m, half, ps, te_tgt):
            ve_t = self.inc("ve", 1)
            bias_ap = self.consts[:, 2 * 16 + m:2 * 16 + m + 1]
            dst = self.vfm[:, m * T + half * 512:m * T + half * 512 + 512]

            def f_v(v):
                v.wait_ge(sm["te"], te_tgt)
                v.tensor_scalar_add(dst, ps, bias_ap).then_inc(sm["ve"], 1)
            self.on("vector", f_v)
        self.w_proj(self.w["wv_l"], 16, 16, rhs, 16, v_evac)

        if self.debug == "qkv":
            self.dump_fm(lambda kc: self.qf[:, kc * T:(kc + 1) * T], NCH, self.dbg[0:D])
            self.dump_fm(lambda kc: self.kf[:, kc * T:(kc + 1) * T], NCH,
                         self.dbg[D:2 * D])
            self.dump_fm(lambda kc: self.vfm[:, kc * T:(kc + 1) * T], NCH,
                         self.dbg[2 * D:3 * D])
            return True
        return False



def build(debug=None):
    b = B(debug)
    b.phase0()
    if b.phase1():
        return b.finish(), b
    if b.phaseA2():
        return b.finish(), b
    if phases_lin(b):
        return b.finish(), b
    if phases_ffn(b):
        return b.finish(), b
    if phases_flash(b):
        return b.finish(), b
    phases_mlp(b)
    return b.finish(), b


U8 = mybir.dt.uint8


def switch_free(b, t, arg, n=2):
    nc = b.nc
    creg = b.cidreg("tensor", t)
    t.reg_alu(b.rpe, creg, arg, ALU.is_le)
    before = nc.next_id()
    sel = make_scalar_value(b.rpe, min_val=0, max_val=n - 1)
    for case in t.Switch(sel, n):
        yield case
    after = nc.next_id()
    for rid in range(before, after):
        for nm in (f"PE_tmp_{rid}", f"PE___switch_{rid}_PE_stride",
                   f"PE___switch_{rid}_PE_delta"):
            try:
                nc.free_register(
                    bass_rust.RegisterHandle(name=nm, engine=mybir.EngineType.PE))
            except Exception:
                pass


def rtap(ap, reg, max_bytes):
    return dataclasses.replace(ap.bitcast(U8),
                               offset=make_scalar_value(reg, min_val=0, max_val=max_bytes))


def phases_lin(b):
    sm = b.sem

    te0, ve0 = b.get("te"), b.get("ve")
    NH = 8 * 16

    def f_te(t):
        for i in range(NH):
            n, h = divmod(i, 16)
            ps = b.psT[i % 2]
            if i >= 2:
                t.wait_ge(sm["ve"], ve0 + 2 * (i - 2) + 1)
            t.transpose(ps[:, 0:128], b.kf[:, h * T + n * 128:h * T + n * 128 + 128],
                        b.ident)
            t.transpose(ps[:, 128:256], b.vfm[:, h * T + n * 128:h * T + n * 128 + 128],
                        b.ident).then_inc(sm["te"], 1)
            t.wait_ge(sm["ve"], ve0 + 2 * i + 1)
            if i >= 2:
                t.wait_ge(sm["ve"], ve0 + 2 * (i - 2) + 2)
            kt = b.rings[:, (i % 2) * 256:(i % 2) * 256 + 128]
            vt = b.vtm[:, i * 128:(i + 1) * 128]
            t.matmul(b.psA[i % 2][:, 0:128], kt, vt, start=True, stop=False)
            t.matmul(b.psA[i % 2][:, 128:129], kt, b.onescol[:, 0:1],
                     start=False, stop=True).then_inc(sm["te"], 1)
    b.on("tensor", f_te)
    b.inc("te", 2 * NH)

    def f_ve(v):
        for i in range(NH):
            n, h = divmod(i, 16)
            ps = b.psT[i % 2]
            v.wait_ge(sm["te"], te0 + 2 * i + 1)
            v.tensor_copy(b.rings[:, (i % 2) * 256:(i % 2) * 256 + 128], ps[:, 0:128])
            v.tensor_copy(b.vtm[:, i * 128:(i + 1) * 128],
                          ps[:, 128:256]).then_inc(sm["ve"], 1)
            v.wait_ge(sm["te"], te0 + 2 * i + 2)
            st_sl = (n // 4) * 2064 + h * 129
            v.tensor_add(b.slot_tot[:, st_sl:st_sl + 129], b.slot_tot[:, st_sl:st_sl + 129],
                         b.psA[i % 2][:, 0:129]).then_inc(sm["ve"], 1)
    b.on("vector", f_ve)
    b.inc("ve", 2 * NH)
    b.barrier()

    def f_sy(sync):
        for s in range(2):
            sync.dma_start(out=b.lin_cin.ap()[s],
                           in_=b.slot_tot[:, s * 2064:(s + 1) * 2064]
                           ).then_inc(sm["d_sy"], 16)
    b.on("sync", f_sy)
    b.inc("d_sy", 32)
    sy_pub = b.get("d_sy")
    s2_pub = b.get("d_s2")

    def f_gp(gp):
        gp.wait_ge(sm["d_sy"], sy_pub)
        gp.collective_compute(
            "AllGather", ALU.bypass, replica_groups=[list(range(8))],
            ins=[b.lin_cin.ap().opt()],
            outs=[b.lin_cout.ap()[0:8].opt()]).then_inc(sm["cc"], 1)
    b.on("gpsimd", f_gp)
    b.inc("cc", 1)
    b.barrier()

    for slot in range(2):
        ve_z = b.inc("ve", 1)

        def f_vz(v):
            v.memset(b.state[:], 0.0).then_inc(sm["ve"], 1)
        b.on("vector", f_vz)

        gp0 = b.get("d_gp")
        ve_p0 = b.get("ve")

        def f_gp2(gp, slot=slot, gp0=gp0, ve_p0=ve_p0):
            creg = b.cidreg("gpsimd", gp)
            r1 = b.rgp[0]
            ezb = 32 * 128 * 2064 * 4
            for j in range(15):
                r = j if j < 8 else 15 - j
                ssl = 0 if j < 8 else 1
                ejb = (r * 2 + ssl) * 128 * 2064 * 4
                if slot == 0:
                    gp.reg_alu(r1, creg, j, ALU.is_gt)
                else:
                    gp.reg_alu(r1, creg, 15 - j, ALU.is_lt)
                gp.reg_alu(r1, r1, ejb - ezb, ALU.mult)
                gp.reg_alu(r1, r1, ezb, ALU.add)
                if j > 0:
                    gp.wait_ge(sm["ve"], ve_p0 + j)
                gp.dma_start(out=b.kvg[:].bitcast(U8),
                             in_=rtap(b.lin_cout.ap()[0, 0], r1, ezb)
                             ).then_inc(sm["d_gp"], 16)
        b.on("gpsimd", f_gp2)
        b.inc("d_gp", 16 * 15)

        def f_vp(v, gp0=gp0, ve_p0=ve_p0, ve_z=ve_z):
            v.wait_ge(sm["ve"], ve_z)
            for j in range(15):
                v.wait_ge(sm["d_gp"], gp0 + 16 * (j + 1))
                v.tensor_add(b.state[:], b.state[:], b.kvg[:]).then_inc(sm["ve"], 1)
        b.on("vector", f_vp)
        b.inc("ve", 15)
        b.barrier()

        for nl in range(4):
            i0 = slot * 4 + nl
            ve_c = b.inc("ve", 1)

            def f_vc(v):
                v.tensor_copy(b.state_bf[:], b.state[:]).then_inc(sm["ve"], 1)
            b.on("vector", f_vc)
            b.barrier()

            te_h0 = b.get("te")
            ve_h0 = b.get("ve")

            def f_teh(t, i0=i0, te_h0=te_h0, ve_h0=ve_h0):
                for h in range(16):
                    col = h * T + i0 * 128
                    vt = b.vtm[:, (i0 * 16 + h) * 128:(i0 * 16 + h) * 128 + 128]
                    at = b.rings[:, 1024 + (h % 2) * 128:1024 + (h % 2) * 128 + 128]
                    kt = b.rings[:, (h % 2) * 256:(h % 2) * 256 + 128]
                    atm = b.rings[:, 1536 + (h % 2) * 128:1536 + (h % 2) * 128 + 128]
                    if h >= 2:
                        t.wait_ge(sm["ve"], ve_h0 + 7 * (h - 2) + 6)
                    t.matmul(b.psA[h % 2][:, 0:128], b.kf[:, col:col + 128],
                             b.qf[:, col:col + 128], start=True, stop=True
                             ).then_inc(sm["te"], 1)
                    if h >= 2:
                        t.wait_ge(sm["ve"], ve_h0 + 7 * (h - 2) + 2)
                    t.transpose(b.psT[h % 2][:, 0:128], b.kf[:, col:col + 128],
                                b.ident).then_inc(sm["te"], 1)
                    msl = (h % 4) * 512
                    t.wait_ge(sm["ve"], ve_h0 + 7 * h + 1)
                    if h >= 4:
                        t.wait_ge(sm["ve"], ve_h0 + 7 * (h - 4) + 5)
                    t.matmul(b.psB[:, msl:msl + 128], at, vt, start=True, stop=False)
                    t.matmul(b.psB[:, msl + 128:msl + 129], at, b.onescol[:, 0:1],
                             start=False, stop=False)
                    t.matmul(b.psB[:, msl:msl + 129], b.qf[:, col:col + 128],
                             b.state_bf[:, h * 129:h * 129 + 129],
                             start=False, stop=True).then_inc(sm["te"], 1)
                    t.wait_ge(sm["ve"], ve_h0 + 7 * h + 2)
                    t.matmul(b.psA[h % 2][:, 0:128], kt, vt, start=True, stop=False)
                    t.matmul(b.psA[h % 2][:, 128:129], kt, b.onescol[:, 0:1],
                             start=False, stop=True).then_inc(sm["te"], 1)
                    t.wait_ge(sm["ve"], ve_h0 + 7 * h + 5)
                    if h >= 2:
                        t.wait_ge(sm["ve"], ve_h0 + 7 * (h - 2) + 7)
                    t.transpose(b.psT[h % 2][:, 128:256], atm,
                                b.ident).then_inc(sm["te"], 1)

            def f_veh(v, i0=i0, te_h0=te_h0, ve_h0=ve_h0):
                for h in range(16):
                    col = h * T + i0 * 128
                    at = b.rings[:, 1024 + (h % 2) * 128:1024 + (h % 2) * 128 + 128]
                    atm = b.rings[:, 1536 + (h % 2) * 128:1536 + (h % 2) * 128 + 128]
                    msl = (h % 4) * 512
                    c0 = 1024 + (h % 2)
                    c1 = 1028 + (h % 2)
                    v.wait_ge(sm["te"], te_h0 + 5 * h + 1)
                    v.tensor_mul(at, b.psA[h % 2][:, 0:128], b.tri).then_inc(sm["ve"], 1)
                    v.wait_ge(sm["te"], te_h0 + 5 * h + 2)
                    v.tensor_copy(b.rings[:, (h % 2) * 256:(h % 2) * 256 + 128],
                                  b.psT[h % 2][:, 0:128]).then_inc(sm["ve"], 1)
                    v.wait_ge(sm["te"], te_h0 + 5 * h + 3)
                    v.tensor_scalar(b.evac[:, c0:c0 + 1], b.psB[:, msl + 128:msl + 129],
                                    EPS_LA, None, ALU.add).then_inc(sm["ve"], 1)
                    v.wait_ge(sm["ve"], ve_h0 + 7 * h + 3)
                    v.reciprocal(b.evac[:, c1:c1 + 1], b.evac[:, c0:c0 + 1]
                                 ).then_inc(sm["ve"], 1)
                    v.wait_ge(sm["ve"], ve_h0 + 7 * h + 4)
                    v.tensor_scalar_mul(atm, b.psB[:, msl:msl + 128],
                                        b.evac[:, c1:c1 + 1]).then_inc(sm["ve"], 1)
                    v.wait_ge(sm["te"], te_h0 + 5 * h + 4)
                    v.tensor_add(b.state[:, h * 129:h * 129 + 129],
                                 b.state[:, h * 129:h * 129 + 129],
                                 b.psA[h % 2][:, 0:129]).then_inc(sm["ve"], 1)
                    v.wait_ge(sm["te"], te_h0 + 5 * h + 5)
                    v.tensor_copy(b.attn[:, col:col + 128],
                                  b.psT[h % 2][:, 128:256]).then_inc(sm["ve"], 1)
            b.on("tensor", f_teh)
            b.on("vector", f_veh)
            b.inc("te", 5 * 16)
            b.inc("ve", 7 * 16)
            b.barrier()

    if b.debug == "attn_lin":
        b.dump_fm(lambda kc: b.attn[:, kc * T:(kc + 1) * T], NCH, b.dbg)
        return True
    return False


def phases_ffn(b):
    sm = b.sem

    def rhs_attn(k, half):
        return b.attn[:, k * T + half * 512:k * T + half * 512 + 512]

    def wo_evac(m, half, ps, te_tgt):
        b.inc("ve", 1)
        bias_ap = b.consts[:, 3 * 16 + m:3 * 16 + m + 1]
        dst = b.wo_sb[:, m * T + half * 512:m * T + half * 512 + 512]

        def f_v(v):
            v.wait_ge(sm["te"], te_tgt)
            v.tensor_scalar_add(dst, ps, bias_ap).then_inc(sm["ve"], 1)
        b.on("vector", f_v)
    b.w_proj(b.w["wo_l"], 16, 16, rhs_attn, 16, wo_evac)

    ln_pass_sb(b, lambda kc: b.wo_sb[:, kc * T:(kc + 1) * T])
    b.ln_rows(extra_rms=False)

    sy0 = [b.get("d_sy"), b.get("d_s2")]
    ve0 = b.get("ve")
    dS = ["d_sy", "d_s2"]

    def f_sy2(sync):
        for kc in range(NCH):
            if kc >= 2:
                sync.wait_ge(sm["ve"], ve0 + 3 * (kc - 2) + 3)
                sync.dma_start(out=b.y_spill.ap()[(kc - 2) * 128:(kc - 1) * 128, :],
                               in_=b.evac[:, (kc % 2) * 1024:(kc % 2) * 1024 + 1024]
                               ).then_inc(sm[dS[kc % 2]], 16)
                sync.wait_ge(sm[dS[kc % 2]], sy0[kc % 2] + 16 * (kc - kc % 2))
            sync.dma_start(out=b.evac[:, (kc % 2) * 1024:(kc % 2) * 1024 + 1024],
                           in_=b.x_fm[kc * 128:(kc + 1) * 128, :]).then_inc(sm[dS[kc % 2]], 16)
        for kc in (NCH - 2, NCH - 1):
            sync.wait_ge(sm["ve"], ve0 + 3 * kc + 3)
            sync.dma_start(out=b.y_spill.ap()[kc * 128:(kc + 1) * 128, :],
                           in_=b.evac[:, (kc % 2) * 1024:(kc % 2) * 1024 + 1024]
                           ).then_inc(sm[dS[kc % 2]], 16)
    b.on("sync", f_sy2)
    b.inc("d_sy", 16 * NCH)
    b.inc("d_s2", 16 * NCH)

    def f_ve2(v):
        for kc in range(NCH):
            wo = b.wo_sb[:, kc * T:(kc + 1) * T]
            pscr = b.psB[:, (kc % 2) * 1024:(kc % 2) * 1024 + 1024]
            nload = kc // 2 + 1
            nstore = max(0, (kc - 2) // 2 + 1) if kc >= 2 else 0
            v.wait_ge(sm[dS[kc % 2]], sy0[kc % 2] + 16 * (nload + nstore))
            v.tensor_sub(pscr, wo, b.bc_mu).then_inc(sm["ve"], 1)
            v.wait_ge(sm["ve"], ve0 + 3 * kc + 1)
            v.tensor_mul(pscr, pscr, b.bc_c1).then_inc(sm["ve"], 1)
            v.wait_ge(sm["ve"], ve0 + 3 * kc + 2)
            v.tensor_add(b.evac[:, (kc % 2) * 1024:(kc % 2) * 1024 + 1024],
                         pscr, b.evac[:, (kc % 2) * 1024:(kc % 2) * 1024 + 1024]
                         ).then_inc(sm["ve"], 1)
    b.on("vector", f_ve2)
    b.inc("ve", 3 * NCH)
    b.barrier()

    if b.debug == "y1":
        dump_hbm(b, b.y_spill.ap(), b.dbg)
        return True

    b.ln_pass(b.y_spill.ap())
    b.ln_rows(extra_rms=True, extra_dsts=True)
    b.ln_norm(b.y_spill.ap(), lambda kc: b.x1[:, kc * T:(kc + 1) * T], gaincol=4)
    if b.debug == "x1":
        b.dump_fm(lambda kc: b.x1[:, kc * T:(kc + 1) * T], NCH, b.dbg)
        return True

    def rhs_x1(k, half):
        return b.x1[:, k * T + half * 512:k * T + half * 512 + 512]

    def mk_copy_evac(dstbuf, scale):
        def evac(m, half, ps, te_tgt):
            b.inc("ve", 1)

            def f_s(s):
                s.wait_ge(sm["te"], te_tgt)
                s.activation(dstbuf[:, m * T + half * 512:m * T + half * 512 + 512], ps,
                             AF.Copy, bias=0.0, scale=scale).then_inc(sm["ve"], 1)
            b.on("scalar", f_s)
        return evac

    b.w_proj(b.w["wq_f"], 16, 16, rhs_x1, 16, mk_copy_evac(b.qf, ISQ))
    rowmax_stats(b, b.qf, b.statq)
    b.w_proj(b.w["wk_f"], 16, 16, rhs_x1, 16, mk_copy_evac(b.kf, 1.0))
    rowmax_stats(b, b.kf, b.statk)

    def f_pk(sync):
        i = 0
        for h in range(16):
            for slot in range(2):
                sync.dma_start(out=b.k_cin.ap()[slot, h],
                               in_=b.kf[:, h * T + slot * 512:h * T + slot * 512 + 512]
                               ).then_inc(sm[["d_sy", "d_s2"][i % 2]], 16)
                i += 1
    b.on("sync", f_pk)
    b.inc("d_sy", 16 * 16)
    b.inc("d_s2", 16 * 16)

    b.w_proj(b.w["wv_f"], 16, 16, rhs_x1, 16, mk_copy_evac(b.vfm, 1.0))

    ve_m = b.inc("ve", 1)

    def f_vm(v):
        v.memset(b.rings[:, 0:1024], 1.0).then_inc(sm["ve"], 1)
    b.on("vector", f_vm)
    b.barrier()
    te0, ve0b = b.get("te"), b.get("ve")
    sy0a = [b.get("d_sy"), b.get("d_s2")]

    def f_tv(t):
        for i in range(128):
            h, tc = divmod(i, 8)
            if i >= 2:
                t.wait_ge(sm["ve"], ve0b + i - 1)
            t.transpose(b.psT[i % 2][:, 0:128],
                        b.vfm[:, h * T + tc * 128:h * T + tc * 128 + 128],
                        b.ident).then_inc(sm["te"], 1)
    b.on("tensor", f_tv)
    b.inc("te", 128)

    def f_vv(v):
        for i in range(128):
            v.wait_ge(sm["te"], te0 + i + 1)
            if i >= 2:
                v.wait_ge(sm[["d_sy", "d_s2"][i % 2]], sy0a[i % 2] + 16 * (i // 2))
            v.tensor_copy(b.rings[:, (i % 2) * 256:(i % 2) * 256 + 128],
                          b.psT[i % 2][:, 0:128]).then_inc(sm["ve"], 1)
    b.on("vector", f_vv)
    b.inc("ve", 128)

    def f_pv(sync):
        for i in range(128):
            h, tc = divmod(i, 8)
            slot, ks = divmod(tc, 4)
            sync.wait_ge(sm["ve"], ve0b + i + 1)
            sync.dma_start(out=b.v_cin.ap()[slot, h, ks],
                           in_=b.rings[:, (i % 2) * 256:(i % 2) * 256 + 129]
                           ).then_inc(sm[["d_sy", "d_s2"][i % 2]], 16)
    b.on("sync", f_pv)
    b.inc("d_sy", 16 * 64)
    b.inc("d_s2", 16 * 64)

    ve_n = b.get("ve")

    def f_pn(sync):
        sync.wait_ge(sm["ve"], ve_n)
        sync.dma_start(out=b.n_cin.ap(), in_=b.statk).then_inc(sm["d_sy"], 16)
    b.on("sync", f_pn)
    b.inc("d_sy", 16)
    b.barrier()

    def f_ag(gp):
        gp.collective_compute(
            "AllGather", ALU.bypass, replica_groups=[list(range(8))],
            ins=[b.n_cin.ap().opt()], outs=[b.n_cout.ap().opt()]).then_inc(sm["cc"], 1)
        gp.collective_compute(
            "AllGather", ALU.bypass, replica_groups=[list(range(8))],
            ins=[b.k_cin.ap().opt()], outs=[b.k_cout.ap().opt()]).then_inc(sm["cc"], 1)
        gp.collective_compute(
            "AllGather", ALU.bypass, replica_groups=[list(range(8))],
            ins=[b.v_cin.ap().opt()], outs=[b.v_cout.ap().opt()]).then_inc(sm["cc"], 1)
    b.on("gpsimd", f_ag)
    b.inc("cc", 3)

    def woa_evac(m, half, ps, te_tgt):
        b.inc("ve", 1)

        def f_v(v):
            v.wait_ge(sm["te"], te_tgt)
            v.tensor_copy(b.woa[:, m * 16:(m + 1) * 16], ps).then_inc(sm["ve"], 1)
        b.on("vector", f_v)
    b.w_proj(b.w["wo_f"], 16, 16, lambda k, half: b.a_sb[:, k * 16:(k + 1) * 16],
             16, woa_evac, mm_n=16, nhalf=1)

    b.ln_norm(b.y_spill.ap(), lambda kc: b.hbuf[:, kc * T:(kc + 1) * T], gaincol=5)

    if b.debug == "qkv_f":
        b.dump_fm(lambda kc: b.qf[:, kc * T:(kc + 1) * T], NCH, b.dbg[0:D])
        b.dump_fm(lambda kc: b.kf[:, kc * T:(kc + 1) * T], NCH, b.dbg[D:2 * D])
        b.dump_fm(lambda kc: b.vfm[:, kc * T:(kc + 1) * T], NCH, b.dbg[2 * D:3 * D])
        return True
    if b.debug == "h2":
        b.dump_fm(lambda kc: b.hbuf[:, kc * T:(kc + 1) * T], NCH, b.dbg)
        return True

    cc_t = b.get("cc")
    sy_n = b.get("d_sy") + 16

    def f_ln(sync):
        sync.wait_ge(sm["cc"], cc_t)
        sync.dma_start(out=b.nstage[0:1, :],
                       in_=b.n_cout.ap().rearrange("r h -> (r h)")[None, :]
                       ).then_inc(sm["d_sy"], 16)
    b.on("sync", f_ln)
    b.inc("d_sy", 16)

    steps = [("vector", lambda v: (v.wait_ge(sm["d_sy"], sy_n),
                                   v.tensor_copy(b.statm, b.nstage[0:1, 0:16]))[1])]
    for r in range(1, 8):
        steps.append(("vector", lambda v, r=r: v.tensor_max(
            b.statm, b.statm, b.nstage[0:1, r * 16:r * 16 + 16])))
    steps += [
        ("vector", lambda v: v.tensor_mul(b.statm, b.statm, b.statq)),
        ("scalar", lambda s: s.activation(b.statm, b.statm, AF.Sqrt, bias=0.0, scale=1.0)),
        ("vector", lambda v: v.tensor_scalar(b.statm, b.statm, -1.0, MARGIN,
                                             ALU.mult, ALU.add)),
        ("vector", lambda v: v.tensor_copy(b.statbf[0:1, 0:16], b.statm)),
        ("tensor", lambda t: t.matmul(b.psA[0][:, 0:16], b.onesrow[0:1, :],
                                      b.statbf[0:1, 0:16], start=True, stop=True)),
        ("vector", lambda v: v.tensor_copy(b.negm, b.psA[0][:, 0:16])),
    ]
    b.row_seq(steps)
    b.barrier()
    return False


def ln_pass_sb(b, chunk_bf):
    sm = b.sem
    se0, te0 = b.get("se"), b.get("te")

    def f_sc(s):
        for kc in range(NCH):
            if kc >= 2:
                s.wait_ge(sm["te"], te0 + kc - 1)
            s.square(b.rings[:, (kc % 2) * 1024:(kc % 2) * 1024 + 1024],
                     chunk_bf(kc)).then_inc(sm["se"], 1)
    b.on("scalar", f_sc)
    b.inc("se", NCH)

    def f_te(t):
        for kc in range(NCH):
            t.wait_ge(sm["se"], se0 + kc + 1)
            last = None
            for half in range(2):
                sl = (kc % 2) * 1024 + half * 512
                t.matmul(b.psB[0:1, half * 512:half * 512 + 512], b.onescol[:, 0:1],
                         chunk_bf(kc)[:, half * 512:half * 512 + 512],
                         start=(kc == 0), stop=(kc == NCH - 1))
                last = t.matmul(b.psB[0:1, 1024 + half * 512:1024 + half * 512 + 512],
                                b.onescol[:, 0:1], b.rings[:, sl:sl + 512],
                                start=(kc == 0), stop=(kc == NCH - 1))
            last.then_inc(sm["te"], 1)
    b.on("tensor", f_te)
    b.inc("te", NCH)


def rowmax_stats(b, buf, statdst):
    sm = b.sem
    se0, te0, ve0 = b.get("se"), b.get("te"), b.get("ve")

    def f_sc(s):
        for h in range(16):
            if h >= 2:
                s.wait_ge(sm["te"], te0 + h - 1)
            s.square(b.rings[:, (h % 2) * 1024:(h % 2) * 1024 + 1024],
                     buf[:, h * T:(h + 1) * T]).then_inc(sm["se"], 1)
    b.on("scalar", f_sc)
    b.inc("se", 16)

    def f_te(t):
        for h in range(16):
            t.wait_ge(sm["se"], se0 + h + 1)
            if h >= 1:
                t.wait_ge(sm["ve"], ve0 + 3 * h - 1)
            sl = (h % 2) * 1024
            t.matmul(b.psA[0][0:1, 0:512], b.onescol[:, 0:1],
                     b.rings[:, sl:sl + 512], start=True, stop=True)
            t.matmul(b.psA[1][0:1, 0:512], b.onescol[:, 0:1],
                     b.rings[:, sl + 512:sl + 1024], start=True, stop=True
                     ).then_inc(sm["te"], 1)
    b.on("tensor", f_te)
    b.inc("te", 16)

    def f_ve(v):
        for h in range(16):
            v.wait_ge(sm["te"], te0 + h + 1)
            if h >= 1:
                v.wait_ge(sm["ve"], ve0 + 3 * h)
            v.reduce_max(b.nstage[0:1, 0:1], b.psA[0][0:1, 0:512], axis=AX.X
                         ).then_inc(sm["ve"], 1)
            v.wait_ge(sm["ve"], ve0 + 3 * h + 1)
            v.reduce_max(b.nstage[0:1, 1:2], b.psA[1][0:1, 0:512], axis=AX.X
                         ).then_inc(sm["ve"], 1)
            v.wait_ge(sm["ve"], ve0 + 3 * h + 2)
            v.tensor_max(statdst[0:1, h:h + 1], b.nstage[0:1, 0:1],
                         b.nstage[0:1, 1:2]).then_inc(sm["ve"], 1)
    b.on("vector", f_ve)
    b.inc("ve", 3 * 16)
    b.barrier()


def dump_hbm(b, src_hbm, dst):
    sm = b.sem
    sy0 = [b.get("d_sy"), b.get("d_s2")]
    dS = ["d_sy", "d_s2"]

    def f_s(sync):
        for kc in range(NCH):
            if kc >= 2:
                sync.wait_ge(sm[dS[kc % 2]], sy0[kc % 2] + 32 * (kc // 2))
            sync.dma_start(out=b.evac[:, (kc % 2) * 1024:(kc % 2) * 1024 + 1024],
                           in_=src_hbm[kc * 128:(kc + 1) * 128, :]).then_inc(sm[dS[kc % 2]], 16)
            sync.wait_ge(sm[dS[kc % 2]], sy0[kc % 2] + 32 * (kc // 2) + 16)
            sync.dma_start(out=dst[kc * 128:(kc + 1) * 128, :],
                           in_=b.evac[:, (kc % 2) * 1024:(kc % 2) * 1024 + 1024]
                           ).then_inc(sm[dS[kc % 2]], 16)
    b.on("sync", f_s)
    b.inc("d_sy", 32 * (NCH // 2))
    b.inc("d_s2", 32 * (NCH // 2))
    b.barrier()


def phases_flash(b):
    sm = b.sem

    ve_ms = b.inc("ve", 1)

    def f_ms(v):
        v.memset(b.modsb[0:16, :], 0.0).then_inc(sm["ve"], 1)
    b.on("vector", f_ms)
    b.barrier()

    iters = [("diag", 0), ("diag", 1)] + [("off", t) for t in range(15)]
    dG = ["d_gp", "d_g2"]

    for h in range(16):
        te0 = b.get("te")
        ve0 = b.get("ve")
        se0 = b.get("se")
        gp0 = [b.get("d_gp"), b.get("d_g2")]

        def te_scores(j, te0=te0):
            return te0 + 2 + 2 * j

        def te_pv(j, te0=te0):
            return te0 + 3 + 2 * j

        def f_gp(gp, h=h, gp0=gp0, te0=te0, te_pv=te_pv):
            creg = b.cidreg("gpsimd", gp)
            r1, r2, r3, r4 = b.rgp
            for it, (kind, arg) in enumerate(iters):
                if kind == "diag":
                    gp.reg_alu(r3, creg, 2, ALU.mult)
                    gp.reg_alu(r3, r3, arg, ALU.add)
                else:
                    t_ = arg
                    gp.reg_alu(r1, creg, t_, ALU.is_gt)
                    gp.reg_alu(r1, r1, -1, ALU.mult)
                    gp.reg_alu(r1, r1, 1, ALU.add)
                    gp.reg_alu(r2, creg, 0, ALU.add)
                    gp.reg_alu(r2, r2, r1, ALU.mult)
                    gp.reg_alu(r2, r2, -1, ALU.mult)
                    gp.reg_alu(r2, r2, t_, ALU.add)
                    gp.reg_alu(r1, r2, 0, ALU.add)
                    gp.reg_alu(r1, r1, 8, ALU.is_ge)
                    gp.reg_alu(r3, r2, -2, ALU.mult)
                    gp.reg_alu(r3, r3, 15, ALU.add)
                    gp.reg_alu(r3, r3, r1, ALU.mult)
                    gp.reg_alu(r3, r3, r2, ALU.add)
                    gp.reg_alu(r3, r3, 2, ALU.mult)
                    gp.reg_alu(r3, r3, r1, ALU.add)
                gp.reg_alu(r3, r3, 16, ALU.mult)
                gp.reg_alu(r3, r3, h, ALU.add)
                gp.reg_alu(r4, r3, 4 * 128 * 129 * 2, ALU.mult)
                gp.reg_alu(r3, r3, 128 * 512 * 2, ALU.mult)
                if it >= 2:
                    gp.wait_ge(sm["te"], te_pv((it - 2) * 4 + 3))
                gp.dma_start(out=b.kbuf[:, (it % 2) * 512:(it % 2) * 512 + 512].bitcast(U8),
                             in_=rtap(b.k_cout.ap()[0, 0, 0], r3, 32 * 16 * 128 * 512 * 2)
                             ).then_inc(sm[dG[it % 2]], 16)
                vdst = b.vbuf[:, (it % 2) * 516:(it % 2) * 516 + 516].rearrange(
                    "p (k c) -> p k c", k=4)
                gp.dma_start(out=vdst.bitcast(U8),
                             in_=rtap(b.v_cout.ap()[0, 0, 0].transpose([1, 0, 2]),
                                      r4, 32 * 16 * 4 * 128 * 129 * 2)
                             ).then_inc(sm[dG[it % 2]], 16)
        b.on("gpsimd", f_gp)
        b.inc("d_gp", 16 * 2 * 9)
        b.inc("d_g2", 16 * 2 * 8)

        def f_te(t, h=h, te0=te0, ve0=ve0, se0=se0, gp0=gp0):
            cid = b.cid("tensor", t)
            last = None
            for bk in range(4):
                last = t.matmul(b.psB[:, bk * 512:bk * 512 + 512], b.zeros128,
                                b.qf[:, 0:512], start=True, stop=False)
            last.then_inc(sm["te"], 1)

            def emit_ks(kind, sel, it, ks):
                j = it * 4 + ks
                W = (4 - ks) * 128 if kind == "diag" else 512
                if j >= 2:
                    t.wait_ge(sm["se"], se0 + j - 1)
                if kind == "diag":
                    qrhs = b.qf[:, h * T + sel * 512 + ks * 128:
                                h * T + sel * 512 + ks * 128 + W]
                else:
                    qrhs = b.qf[:, h * T + sel * 512:h * T + sel * 512 + W]
                t.matmul(b.psA[ks % 2][:, 0:W],
                         b.kbuf[:, (it % 2) * 512 + ks * 128:(it % 2) * 512 + ks * 128 + 128],
                         qrhs, start=True, stop=True).then_inc(sm["te"], 1)
                if kind == "diag":
                    t.wait_ge(sm["ve"], ve0 + j + 1)
                else:
                    t.wait_ge(sm["se"], se0 + j + 1)
                pslot = (j % 2) * 512
                last = None
                for qs in range(W // 128):
                    qsub = qs + (ks if kind == "diag" else 0)
                    out = b.psB[:, sel * 1024 + qsub * 256:sel * 1024 + qsub * 256 + 129]
                    last = t.matmul(out,
                                    b.probs[:, pslot + qs * 128:pslot + qs * 128 + 128],
                                    b.vbuf[:, (it % 2) * 516 + ks * 129:
                                           (it % 2) * 516 + ks * 129 + 129],
                                    start=False, stop=False)
                last.then_inc(sm["te"], 1)

            for it, (kind, arg) in enumerate(iters):
                npair = it // 2 + 1
                t.wait_ge(sm[dG[it % 2]], gp0[it % 2] + 32 * npair)
                if kind == "diag":
                    for ks in range(4):
                        emit_ks("diag", arg, it, ks)
                else:
                    for case in switch_free(b, t, arg, 2):
                        for ks in range(4):
                            emit_ks("off", case, it, ks)
            for bk in range(4):
                t.matmul(b.psB[:, bk * 512 + 500:bk * 512 + 501], b.zeros128,
                         b.qf[:, 0:1], start=False, stop=True)
        b.on("tensor", f_te)
        b.inc("te", 1 + 17 * 8)

        def f_sc(s, h=h, te0=te0, se0=se0, te_scores=te_scores, te_pv=te_pv):
            for it, (kind, arg) in enumerate(iters):
                for ks in range(4):
                    j = it * 4 + ks
                    W = (4 - ks) * 128 if kind == "diag" else 512
                    s.wait_ge(sm["te"], te_scores(j))
                    if j >= 2:
                        s.wait_ge(sm["te"], te_pv(j - 2))
                    pslot = (j % 2) * 512
                    s.activation(b.probs[:, pslot:pslot + W], b.psA[ks % 2][:, 0:W],
                                 AF.Exp, bias=b.negm[:, h:h + 1], scale=1.0
                                 ).then_inc(sm["se"], 1)
        b.on("scalar", f_sc)
        b.inc("se", 17 * 4)

        def f_vm2(v, se0=se0):
            for j in range(8):
                v.wait_ge(sm["se"], se0 + j + 1)
                pslot = (j % 2) * 512
                v.tensor_mul(b.probs[:, pslot:pslot + 128],
                             b.probs[:, pslot:pslot + 128], b.tri).then_inc(sm["ve"], 1)
        b.on("vector", f_vm2)
        b.inc("ve", 8)

        te_all = te0 + 1 + 17 * 8
        ve_e0 = ve0 + 8

        def f_ev(v, h=h, te_all=te_all, ve_e0=ve_e0):
            v.wait_ge(sm["te"], te_all)
            for k in range(8):
                sel, qs = divmod(k, 4)
                ocol = sel * 1024 + qs * 256
                v.reciprocal(b.evac[:, 1024 + k % 2:1024 + k % 2 + 1],
                             b.psB[:, ocol + 128:ocol + 129]).then_inc(sm["ve"], 1)
                v.wait_ge(sm["ve"], ve_e0 + 3 * k + 1)
                v.tensor_scalar_mul(b.rings[:, 1536 + (k % 2) * 128:1536 + (k % 2) * 128 + 128],
                                    b.psB[:, ocol:ocol + 128],
                                    b.evac[:, 1024 + k % 2:1024 + k % 2 + 1]
                                    ).then_inc(sm["ve"], 1)
                v.wait_ge(sm["te"], te_all + k + 1)
                v.tensor_copy(b.headring[:, (h % 2) * 1024 + sel * 512 + qs * 128:
                                         (h % 2) * 1024 + sel * 512 + qs * 128 + 128],
                              b.psT[k % 2][:, 0:128]).then_inc(sm["ve"], 1)
        b.on("vector", f_ev)
        b.inc("ve", 24)

        def f_evt(t, te_all=te_all, ve_e0=ve_e0):
            for k in range(8):
                t.wait_ge(sm["ve"], ve_e0 + 3 * k + 2)
                if k >= 2:
                    t.wait_ge(sm["ve"], ve_e0 + 3 * (k - 2) + 3)
                t.transpose(b.psT[k % 2][:, 0:128],
                            b.rings[:, 1536 + (k % 2) * 128:1536 + (k % 2) * 128 + 128],
                            b.ident).then_inc(sm["te"], 1)
        b.on("tensor", f_evt)
        b.inc("te", 8)

        ve_hr = b.get("ve")
        te_m0 = b.get("te")

        def f_mod_t(t, h=h, ve_hr=ve_hr):
            t.wait_ge(sm["ve"], ve_hr)
            t.matmul(b.psA[0][0:16, 0:512], b.woa[:, h * 16:h * 16 + 16],
                     b.headring[:, (h % 2) * 1024:(h % 2) * 1024 + 512],
                     start=True, stop=True).then_inc(sm["te"], 1)
            t.matmul(b.psA[1][0:16, 0:512], b.woa[:, h * 16:h * 16 + 16],
                     b.headring[:, (h % 2) * 1024 + 512:(h % 2) * 1024 + 1024],
                     start=True, stop=True).then_inc(sm["te"], 1)
        b.on("tensor", f_mod_t)
        b.inc("te", 2)

        def f_mod_v(v, te_m0=te_m0):
            v.wait_ge(sm["te"], te_m0 + 1)
            v.tensor_add(b.modsb[0:16, 0:512], b.modsb[0:16, 0:512],
                         b.psA[0][0:16, 0:512]).then_inc(sm["ve"], 1)
            v.wait_ge(sm["te"], te_m0 + 2)
            v.tensor_add(b.modsb[0:16, 512:1024], b.modsb[0:16, 512:1024],
                         b.psA[1][0:16, 0:512]).then_inc(sm["ve"], 1)
        b.on("vector", f_mod_v)
        b.inc("ve", 2)
        b.barrier()

    b.row_seq([
        ("scalar", lambda s: s.activation(b.modsb[0:16, :], b.modsb[0:16, :],
                                          AF.Tanh, bias=0.0, scale=1.0)),
        ("vector", lambda v: v.tensor_scalar_mul(b.modsb[0:16, :], b.modsb[0:16, :], 0.1)),
    ])
    b.barrier()
    if b.debug == "mod":
        sy_d = b.get("d_sy")
        ve_d = b.inc("ve", 1)

        def f_dv(v):
            v.tensor_copy(b.evac[0:16, 0:1024], b.modsb[0:16, :]).then_inc(sm["ve"], 1)
        b.on("vector", f_dv)

        def f_ds(sync):
            sync.wait_ge(sm["ve"], ve_d)
            sync.dma_start(out=b.dbg, in_=b.evac[0:16, 0:1024]).then_inc(sm["d_sy"], 16)
        b.on("sync", f_ds)
        b.inc("d_sy", 16)
        b.barrier()
        return True
    return False


def phases_mlp(b):
    sm = b.sem

    ve_z = b.inc("ve", 1)

    def f_z(v):
        v.memset(b.modha[:], 0.0).then_inc(sm["ve"], 1)
    b.on("vector", f_z)
    b.barrier()
    te0 = b.get("te")
    ve0 = b.get("ve")

    def f_ha_t(t, te0=te0, ve0=ve0):
        for half in range(2):
            last = None
            for k in range(16):
                last = t.matmul(b.psA[half][0:16, 0:512], b.a_sb[:, k * 16:(k + 1) * 16],
                                b.hbuf[:, k * T + half * 512:k * T + half * 512 + 512],
                                start=(k == 0), stop=(k == 15))
            last.then_inc(sm["te"], 1)
    b.on("tensor", f_ha_t)
    b.inc("te", 2)

    def f_ha_v(v, te0=te0):
        for half in range(2):
            v.wait_ge(sm["te"], te0 + half + 1)
            v.tensor_mul(b.modha[0:16, half * 512:half * 512 + 512],
                         b.psA[half][0:16, 0:512],
                         b.modsb[0:16, half * 512:half * 512 + 512]).then_inc(sm["ve"], 1)
    b.on("vector", f_ha_v)
    b.inc("ve", 2)
    b.barrier()

    te0 = b.get("te")
    ve0 = b.get("ve")
    se0 = b.get("se")
    gp0 = [b.get("d_gp"), b.get("d_g2")]
    dG = ["d_gp", "d_g2"]

    def f_gp(gp, gp0=gp0, te0=te0):
        i = 0
        for f in range(64):
            for which, src, ntile in (("g", b.w["w_g"], 17), ("u", b.w["w_u"], 16)):
                if i >= 2:
                    gp.wait_ge(sm["te"], te0 + i - 1)
                gp.dma_start(out=b.wring[:, (i % 2) * 2176:(i % 2) * 2176 + ntile * 128],
                             in_=src[f, :, 0:ntile * 128]).then_inc(sm[dG[i % 2]], 16)
                i += 1
    b.on("gpsimd", f_gp)
    b.inc("d_gp", 16 * 64)
    b.inc("d_g2", 16 * 64)

    def f_te(t, gp0=gp0, te0=te0, ve0=ve0, se0=se0):
        i = 0
        for f in range(64):
            for wi, ntile in ((0, 17), (1, 16)):
                t.wait_ge(sm[dG[i % 2]], gp0[i % 2] + 16 * (i // 2 + 1))
                if f >= 1 and wi == 0:
                    t.wait_ge(sm["se"], se0 + (f - 1) * 2 + 2)
                if f >= 1 and wi == 1:
                    t.wait_ge(sm["ve"], ve0 + (f - 1) * 2 + 2)
                slot = (i % 2) * 2176
                last = None
                for half in range(2):
                    out = b.psB[:, wi * 1024 + half * 512:wi * 1024 + half * 512 + 512]
                    for k in range(16):
                        last = t.matmul(out, b.wring[:, slot + k * 128:slot + (k + 1) * 128],
                                        b.hbuf[:, k * T + half * 512:k * T + half * 512 + 512],
                                        start=(k == 0), stop=(k == 15 and ntile == 16))
                    if ntile == 17:
                        last = t.matmul(out, b.wring[:, slot + 16 * 128:slot + 17 * 128],
                                        b.modha[:, half * 512:half * 512 + 512],
                                        start=False, stop=True)
                last.then_inc(sm["te"], 1)
                i += 1
    b.on("tensor", f_te)
    b.inc("te", 128)

    def f_silu(s, te0=te0, se0=se0, ve0=ve0):
        for f in range(64):
            for half in range(2):
                s.wait_ge(sm["te"], te0 + f * 2 + 1)
                if f >= 1:
                    s.wait_ge(sm["ve"], ve0 + (f - 1) * 2 + half + 1)
                gate_ps = b.psB[:, half * 512:half * 512 + 512]
                s.activation(b.ebf[:, half * 512:half * 512 + 512],
                             gate_ps, AF.Silu, bias=0.0, scale=1.0).then_inc(sm["se"], 1)
    b.on("scalar", f_silu)
    b.inc("se", 128)

    def f_ev(v, te0=te0, ve0=ve0, se0=se0):
        for f in range(64):
            for half in range(2):
                up_ps = b.psB[:, 1024 + half * 512:1024 + half * 512 + 512]
                v.wait_ge(sm["se"], se0 + f * 2 + half + 1)
                v.wait_ge(sm["te"], te0 + f * 2 + 2)
                v.tensor_mul(b.gu[:, f * T + half * 512:f * T + half * 512 + 512],
                             b.ebf[:, half * 512:half * 512 + 512],
                             up_ps).then_inc(sm["ve"], 1)
    b.on("vector", f_ev)
    b.inc("ve", 128)
    b.barrier()

    te0 = b.get("te")
    ve0 = b.get("ve")
    se0 = b.get("se")
    gp0 = [b.get("d_gp"), b.get("d_g2")]
    sy0 = [b.get("d_sy"), b.get("d_s2")]
    dS = ["d_sy", "d_s2"]

    def f_gp2(gp, gp0=gp0, te0=te0):
        i = 0
        for m in range(16):
            for sub in range(4):
                if i >= 2:
                    gp.wait_ge(sm["te"], te0 + (i - 1))
                gp.dma_start(out=b.wring[:, (i % 2) * 2176:(i % 2) * 2176 + 2048],
                             in_=b.w["w_dn"][m, :, sub * 2048:(sub + 1) * 2048]
                             ).then_inc(sm[dG[i % 2]], 16)
                i += 1
    b.on("gpsimd", f_gp2)
    b.inc("d_gp", 16 * 32)
    b.inc("d_g2", 16 * 32)

    def f_te2(t, gp0=gp0, te0=te0, ve0=ve0):
        i = 0
        for m in range(16):
            if m >= 1:
                t.wait_ge(sm["ve"], ve0 + (m - 1) * 4 + 4)
            for sub in range(4):
                t.wait_ge(sm[dG[i % 2]], gp0[i % 2] + 16 * (i // 2 + 1))
                slot = (i % 2) * 2176
                last = None
                for half in range(2):
                    out = b.psA[half][:, 0:512]
                    for k in range(16):
                        kk = sub * 16 + k
                        last = t.matmul(out, b.wring[:, slot + k * 128:slot + (k + 1) * 128],
                                        b.gu[:, kk * T + half * 512:kk * T + half * 512 + 512],
                                        start=(kk == 0), stop=(kk == 63))
                last.then_inc(sm["te"], 1)
                i += 1
    b.on("tensor", f_te2)
    b.inc("te", 64)

    def f_sy(sync, ve0=ve0, sy0=sy0):
        for m in range(16):
            if m >= 2:
                sync.wait_ge(sm["ve"], ve0 + (m - 2) * 4 + 4)
            sync.dma_start(out=b.evac[:, (m % 2) * 1024:(m % 2) * 1024 + 1024],
                           in_=b.y_spill.ap()[m * 128:(m + 1) * 128, :]
                           ).then_inc(sm[dS[m % 2]], 16)
    b.on("sync", f_sy)
    b.inc("d_sy", 16 * 8)
    b.inc("d_s2", 16 * 8)

    def f_ve2(v, te0=te0, ve0=ve0, sy0=sy0, se0=se0):
        for m in range(16):
            sl = (m % 2) * 1024
            v.wait_ge(sm[dS[m % 2]], sy0[m % 2] + 16 * (m // 2 + 1))
            v.tensor_sub(b.evac[:, sl:sl + 1024], b.evac[:, sl:sl + 1024],
                         b.bc_mu8).then_inc(sm["ve"], 1)
            v.wait_ge(sm["ve"], ve0 + m * 4 + 1)
            v.tensor_mul(b.evac[:, sl:sl + 1024], b.evac[:, sl:sl + 1024],
                         b.bc_s38).then_inc(sm["ve"], 1)
            for half in range(2):
                v.wait_ge(sm["te"], te0 + m * 4 + 4)
                v.wait_ge(sm["ve"], ve0 + m * 4 + 2 + half)
                v.tensor_add(b.frbuf[:, m * T + half * 512:m * T + half * 512 + 512],
                             b.evac[:, sl + half * 512:sl + half * 512 + 512],
                             b.psA[half][:, 0:512]).then_inc(sm["ve"], 1)
    b.on("vector", f_ve2)
    b.inc("ve", 16 * 4)

    def f_sq(s, ve0=ve0, se0=se0, te0=te0):
        for m in range(16):
            s.wait_ge(sm["ve"], ve0 + m * 4 + 4)
            if m >= 2:
                s.wait_ge(sm["te"], te0 + 64 + m - 1)
            s.square(b.rings[:, (m % 2) * 1024:(m % 2) * 1024 + 1024],
                     b.frbuf[:, m * T:(m + 1) * T]).then_inc(sm["se"], 1)
    b.on("scalar", f_sq)
    b.inc("se", 16)

    def f_st(t, se0=se0, te0=te0):
        for m in range(16):
            t.wait_ge(sm["se"], se0 + m + 1)
            last = None
            for half in range(2):
                sl = (m % 2) * 1024 + half * 512
                t.matmul(b.psB[0:1, half * 512:half * 512 + 512], b.onescol[:, 0:1],
                         b.frbuf[:, m * T + half * 512:m * T + half * 512 + 512],
                         start=(m == 0), stop=(m == 15))
                last = t.matmul(b.psB[0:1, 1024 + half * 512:1024 + half * 512 + 512],
                                b.onescol[:, 0:1], b.rings[:, sl:sl + 512],
                                start=(m == 0), stop=(m == 15))
            last.then_inc(sm["te"], 1)
    b.on("tensor", f_st)
    b.inc("te", 16)
    b.barrier()

    b.ln_rows(extra_rms=False)

    sy0 = [b.get("d_sy"), b.get("d_s2")]
    ve0 = b.get("ve")

    def f_syf(sync, sy0=sy0, ve0=ve0):
        for kc in range(NCH):
            if kc >= 2:
                sync.wait_ge(sm["ve"], ve0 + 3 * (kc - 2) + 3)
                sync.dma_start(out=b.out_fm[(kc - 2) * 128:(kc - 1) * 128, :],
                               in_=b.evac[:, ((kc - 2) % 2) * 1024:((kc - 2) % 2) * 1024 + 1024]
                               ).then_inc(sm[dS[kc % 2]], 16)
                sync.wait_ge(sm[dS[kc % 2]], sy0[kc % 2] + 16 * (kc - kc % 2))
            sync.dma_start(out=b.evac[:, (kc % 2) * 1024:(kc % 2) * 1024 + 1024],
                           in_=b.y_spill.ap()[kc * 128:(kc + 1) * 128, :]
                           ).then_inc(sm[dS[kc % 2]], 16)
        for kc in (NCH - 2, NCH - 1):
            sync.wait_ge(sm["ve"], ve0 + 3 * kc + 3)
            sync.dma_start(out=b.out_fm[kc * 128:(kc + 1) * 128, :],
                           in_=b.evac[:, (kc % 2) * 1024:(kc % 2) * 1024 + 1024]
                           ).then_inc(sm[dS[kc % 2]], 16)
    b.on("sync", f_syf)
    b.inc("d_sy", 16 * NCH)
    b.inc("d_s2", 16 * NCH)

    def f_vef(v, sy0=sy0, ve0=ve0):
        for kc in range(NCH):
            sl = (kc % 2) * 1024
            nload = kc // 2 + 1
            nstore = max(0, (kc - 2) // 2 + 1) if kc >= 2 else 0
            v.wait_ge(sm[dS[kc % 2]], sy0[kc % 2] + 16 * (nload + nstore))
            ps2 = b.psB[:, (kc % 2) * 1024:(kc % 2) * 1024 + 1024]
            v.tensor_sub(ps2, b.frbuf[:, kc * T:(kc + 1) * T], b.bc_mu).then_inc(sm["ve"], 1)
            v.wait_ge(sm["ve"], ve0 + 3 * kc + 1)
            v.tensor_mul(ps2, ps2, b.bc_c1).then_inc(sm["ve"], 1)
            v.wait_ge(sm["ve"], ve0 + 3 * kc + 2)
            v.tensor_add(b.evac[:, sl:sl + 1024], ps2,
                         b.evac[:, sl:sl + 1024]).then_inc(sm["ve"], 1)
    b.on("vector", f_vef)
    b.inc("ve", 3 * NCH)
    b.barrier()


def kernel(**inputs):
    maps = host_inputs(inputs)
    nc, b = build(debug=None)
    res = run_bass_kernel_spmd(nc, maps, core_ids=list(range(NCORES)))
    return host_output(res.results)
